# version 01
# baseline (speedup 1.0000x reference)
import contextlib
import numpy as np
import ml_dtypes

import concourse.bass as bass
import concourse.mybir as mybir
from concourse.bass_utils import run_bass_kernel_spmd

F32 = mybir.dt.float32
BF16 = mybir.dt.bfloat16
AF = mybir.ActivationFunctionType
ALU = mybir.AluOpType
AX = mybir.AxisListType
BFD = ml_dtypes.bfloat16

B, S, DM = 4, 2048, 2048
NH, HD, NS, LCH = 64, 64, 128, 128
INTER = 4096
GN = NS
CONV_DIM = INTER + 2 * GN
D_IN = 2 * INTER + 2 * GN + NH
NCH = S // LCH
EPS = 1e-5
EPS32 = float(np.finfo(np.float32).eps)


class SemT:
    __slots__ = ("h", "n")

    def __init__(self, h):
        self.h = h
        self.n = 0


class Plan:
    ENGINES = ("sync", "gpsimd", "tensor", "scalar", "vector")

    def __init__(self, nc):
        self.nc = nc
        self.q = {e: [] for e in self.ENGINES}
        self.stack = contextlib.ExitStack()

    def sem(self, name):
        return SemT(self.stack.enter_context(self.nc.semaphore(name)))

    def dma(self, eng, out, in_, sem, **kw):
        def f(e, out=out, in_=in_, kw=kw, sem=sem):
            e.dma_start(out=out, in_=in_, **kw).then_inc(sem.h, 16)
        self.q[eng].append(f)
        sem.n += 16
        return sem.n

    def wait(self, eng, sem, val=None):
        v = sem.n if val is None else val
        if v <= 0:
            return
        self.q[eng].append(lambda e, s=sem, v=v: e.wait_ge(s.h, v))

    def op(self, eng, fn, sem=None):
        if sem is None:
            self.q[eng].append(lambda e, fn=fn: fn(e))
            return None
        def f(e, fn=fn, sem=sem):
            fn(e).then_inc(sem.h, 1)
        self.q[eng].append(f)
        sem.n += 1
        return sem.n

    def build(self):
        nc = self.nc
        with nc.Block() as blk:
            for eng in self.ENGINES:
                ops = self.q[eng]
                if not ops:
                    continue
                def section(e, ops=ops):
                    for f in ops:
                        f(e)
                getattr(blk, eng)(section)
        self.stack.close()
        return nc


def bfc(x):
    return np.ascontiguousarray(np.asarray(x)).astype(BFD)


def f32c(x):
    return np.ascontiguousarray(np.asarray(x), dtype=np.float32)



def emit_ln(P, g, src, nch, dst, *, tok, w_col=None, b_col=None, res_src=None,
            res_dst=None, res_bf=None, eps_col=1):
    s_t, s_sc, s_v = g["s_t"], g["s_sc"], g["s_v"]
    rows, xsq, con = g["rows"], g["xsq"], g["con"]
    PBrow, PBC = g["PBrow"], g["PBC"]
    nwin = (tok + 511) // 512
    inv = 1.0 / (nch * 128)
    for w in range(nwin):
        t0, t1 = w * 512, min(tok, w * 512 + 512)
        tw = t1 - t0
        P.wait("tensor", P_sv_obj := g["s_v"], g["s_v"].n)
        for c in range(nch):
            P.op("tensor", lambda t, c=c, t0=t0, t1=t1, tw=tw: t.matmul(
                PBrow[0:1, 0:tw], g["ones_colb"], src(c)[:, t0:t1],
                start=(c == 0), stop=(c == nch - 1)), sem=None)
        last_mm = [None]
        for c in range(nch):
            sq = P.op("scalar", lambda s, c=c, t0=t0, t1=t1, tw=tw: s.activation(
                xsq[:, (c % 2) * 512:(c % 2) * 512 + tw], src(c)[:, t0:t1], AF.Square),
                sem=s_sc)
            P.wait("tensor", s_sc, sq)
            mm = P.op("tensor", lambda t, c=c, tw=tw: t.matmul(
                PBrow[0:1, 512:512 + tw], g["ones_colb"],
                xsq[:, (c % 2) * 512:(c % 2) * 512 + tw],
                start=(c == 0), stop=(c == nch - 1)), sem=s_t)
            if c >= 1:
                pass
            if last_mm[0] is not None:
                P.wait("scalar", s_t, last_mm[0])
            last_mm[0] = mm
        mm_done = s_t.n
        P.wait("vector", s_t, mm_done)
        P.op("vector", lambda v, tw=tw, inv=inv: v.tensor_single_scalar(
            rows[0:1, 0, 0:tw], PBrow[0:1, 0:tw], inv, ALU.mult), sem=None)
        P.op("vector", lambda v, tw=tw, inv=inv: v.tensor_single_scalar(
            rows[0:1, 1, 0:tw], PBrow[0:1, 512:512 + tw], inv, ALU.mult), sem=None)
        P.op("vector", lambda v, tw=tw: v.tensor_tensor(
            rows[0:1, 2, 0:tw], rows[0:1, 0, 0:tw], rows[0:1, 0, 0:tw], ALU.mult), sem=None)
        P.op("vector", lambda v, tw=tw: v.tensor_tensor(
            rows[0:1, 1, 0:tw], rows[0:1, 1, 0:tw], rows[0:1, 2, 0:tw], ALU.subtract), sem=s_v)
        rv = s_v.n
        P.wait("scalar", s_v, rv)
        P.op("scalar", lambda s, tw=tw: s.activation(
            rows[0:1, 2, 0:tw], rows[0:1, 1, 0:tw], AF.Ln, bias=con[0:1, eps_col:eps_col + 1]), sem=None)
        P.op("scalar", lambda s, tw=tw: s.activation(
            rows[0:1, 1, 0:tw], rows[0:1, 2, 0:tw], AF.Exp, scale=-0.5), sem=s_sc)
        rr = s_sc.n
        P.wait("vector", s_sc, rr)
        P.op("vector", lambda v, tw=tw: v.tensor_tensor(
            rows[0:1, 2, 0:tw], rows[0:1, 0, 0:tw], rows[0:1, 1, 0:tw], ALU.mult), sem=None)
        P.op("vector", lambda v, tw=tw: v.tensor_single_scalar(
            rows[0:1, 2, 0:tw], rows[0:1, 2, 0:tw], -1.0, ALU.mult), sem=s_v)
        rc = s_v.n
        P.wait("tensor", s_v, rc)
        P.op("tensor", lambda t, tw=tw: t.matmul(
            PBC[:, 0:tw], g["ones_row"], rows[0:1, 1, 0:tw], start=True, stop=True), sem=None)
        bc = P.op("tensor", lambda t, tw=tw: t.matmul(
            PBC[:, 512:512 + tw], g["ones_row"], rows[0:1, 2, 0:tw], start=True, stop=True), sem=s_t)
        P.wait("vector", s_t, bc)
        for c in range(nch):
            P.op("vector", lambda v, c=c, t0=t0, t1=t1, tw=tw: v.scalar_tensor_tensor(
                g["lnsc"][:, 0:tw], src(c)[:, t0:t1], 1.0, PBC[:, 0:tw],
                ALU.mult, ALU.mult), sem=None)
            if w_col is None and res_src is None:
                P.op("vector", lambda v, c=c, t0=t0, t1=t1, tw=tw: v.tensor_tensor(
                    dst(c, t0, t1), g["lnsc"][:, 0:tw], PBC[:, 512:512 + tw], ALU.add), sem=s_v)
            elif w_col is not None:
                P.op("vector", lambda v, c=c, t0=t0, t1=t1, tw=tw: v.tensor_tensor(
                    g["lnsc"][:, 0:tw], g["lnsc"][:, 0:tw], PBC[:, 512:512 + tw], ALU.add), sem=None)
                P.op("vector", lambda v, c=c, t0=t0, t1=t1, tw=tw: v.tensor_scalar(
                    dst(c, t0, t1), g["lnsc"][:, 0:tw], w_col(c), b_col(c),
                    ALU.mult, ALU.add), sem=s_v)
            else:
                P.op("vector", lambda v, c=c, t0=t0, t1=t1, tw=tw: v.tensor_tensor(
                    g["lnsc"][:, 0:tw], g["lnsc"][:, 0:tw], PBC[:, 512:512 + tw], ALU.add), sem=None)
                P.op("vector", lambda v, c=c, t0=t0, t1=t1, tw=tw: v.tensor_tensor(
                    res_dst(c, t0, t1), res_src(c, t0, t1), g["lnsc"][:, 0:tw], ALU.add), sem=s_v)
                if res_bf is not None:
                    P.op("vector", lambda v, c=c, t0=t0, t1=t1, tw=tw: v.tensor_copy(
                        res_bf(c, t0, t1), res_dst(c, t0, t1)), sem=s_v)
    return s_v.n



AB_DEBUG = False


def build_AB():
    nc = bass.Bass("TRN2", target_bir_lowering=False)
    P = Plan(nc)
    TOK = S

    xT = nc.dram_tensor("xT", [DM, TOK], F32, kind="ExternalInput").ap()
    w_in_sl = nc.dram_tensor("w_in_sl", [DM, 4416], F32, kind="ExternalInput").ap()
    conv_w_sl = nc.dram_tensor("conv_w_sl", [128, 18, 4], F32, kind="ExternalInput").ap()
    conv_b_sl = nc.dram_tensor("conv_b_sl", [128, 18], F32, kind="ExternalInput").ap()
    dt_bias_c = nc.dram_tensor("dt_bias_c", [64, 1], F32, kind="ExternalInput").ap()
    A_c = nc.dram_tensor("A_c", [64, 1], F32, kind="ExternalInput").ap()
    D_rep = nc.dram_tensor("D_rep", [1, 2048], F32, kind="ExternalInput").ap()
    ident = nc.dram_tensor("ident", [128, 128], F32, kind="ExternalInput").ap()
    identb = nc.dram_tensor("identb", [128, 128], BF16, kind="ExternalInput").ap()
    consts = nc.dram_tensor("consts", [128, 8], F32, kind="ExternalInput").ap()
    onesrow = nc.dram_tensor("onesrow", [1, 128], F32, kind="ExternalInput").ap()
    onescolb = nc.dram_tensor("onescolb", [128, 1], BF16, kind="ExternalInput").ap()
    stkL_init = nc.dram_tensor("stkL_init", [48, 4096], F32, kind="ExternalInput").ap()
    stkR_init = nc.dram_tensor("stkR_init", [48, 4096], F32, kind="ExternalInput").ap()

    z_out = nc.dram_tensor("z_out", [16, 128, TOK], BF16, kind="ExternalOutput").ap()
    if AB_DEBUG:
        dbg_ybar = nc.dram_tensor("dbg_ybar", [16, 128, TOK], BF16, kind="ExternalOutput").ap()
        dbg_xc = nc.dram_tensor("dbg_xc", [18, 128, TOK], BF16, kind="ExternalOutput").ap()
        dbg_dt = nc.dram_tensor("dbg_dt", [64, TOK], F32, kind="ExternalOutput").ap()
        dbg_cA = nc.dram_tensor("dbg_cA", [64, TOK], F32, kind="ExternalOutput").ap()
        dbg_M = nc.dram_tensor("dbg_M", [128, 4096], BF16, kind="ExternalOutput").ap()
        dbg_ych = nc.dram_tensor("dbg_ych", [2, 128, 2048], BF16, kind="ExternalOutput").ap()
        dbg_xraw = nc.dram_tensor("dbg_xraw", [128, 2048], BF16, kind="ExternalOutput").ap()
        dbg_xcT = nc.dram_tensor("dbg_xcT", [128, 2048], BF16, kind="ExternalOutput").ap()
        dbg_vsb = nc.dram_tensor("dbg_vsb", [128, 2048], BF16, kind="ExternalOutput").ap()
        dbg_drepb = nc.dram_tensor("dbg_drepb", [128, 2048], BF16, kind="ExternalOutput").ap()
        dbg_ydr = nc.dram_tensor("dbg_ydr", [16, 128, TOK], BF16, kind="ExternalOutput").ap()

    ybar_sp = nc.dram_tensor("ybar_sp", [16, 128, TOK], BF16, kind="Internal").ap()
    y_dr = nc.dram_tensor("y_dr", [16, 128, TOK], BF16, kind="Internal").ap()
    stkL = nc.dram_tensor("stkL", [48, 4096], F32, kind="Internal").ap()
    stkR = nc.dram_tensor("stkR", [48, 4096], F32, kind="Internal").ap()
    cd_dr = nc.dram_tensor("cd_dr", [32, 16], F32, kind="Internal").ap()

    rem = nc.sbuf_bytes_remaining
    arena_h = nc.alloc_sbuf_tensor("arena", [128, ((rem - 1024) // 512) * 512], mybir.dt.uint8)
    ar_f32 = arena_h.ap().bitcast(F32)
    ar_bf = arena_h.ap().bitcast(BF16)

    def cv(off, dtype, shape):
        parts = shape[0]
        elems = 1
        for d in shape[1:]:
            elems *= d
        base = ar_f32 if dtype == F32 else ar_bf
        isz = 4 if dtype == F32 else 2
        v = base[0:parts, off // isz: off // isz + elems]
        if len(shape) == 2:
            return v
        if len(shape) == 3:
            return v.rearrange("p (a b) -> p a b", a=shape[1], b=shape[2])
        if len(shape) == 4:
            return v.rearrange("p (a b c) -> p a b c", a=shape[1], b=shape[2], c=shape[3])
        raise ValueError(shape)

    KB = 1024
    XC = 0
    RA = 72 * KB
    RB = RA + 64 * KB
    PERS = RB + 64 * KB
    xc = cv(XC, BF16, [128, 18, TOK])
    rows = cv(XC, F32, [1, 6, 512])
    xsq = cv(XC + 12 * KB, BF16, [128, 1024])
    lnsc = cv(XC + 14 * KB, F32, [128, 512])
    xb = cv(RA, BF16, [128, 16, TOK])
    xraw = cv(RA, BF16, [128, TOK])
    craw = cv(RA + 4 * KB, F32, [128, TOK])
    wcol = cv(RA + 12 * KB, BF16, [128, 2, 16, 128])
    dt0 = cv(RA + 20 * KB, F32, [64, TOK])
    dt1 = cv(RA + 28 * KB, F32, [64, TOK])
    dt2 = cv(RA + 36 * KB, F32, [64, TOK])
    dt3 = cv(RA + 44 * KB, F32, [64, TOK])
    drow = cv(RA + 52 * KB, F32, [1, 2048])
    usb = cv(RA, BF16, [128, 2048])
    vsb = cv(RA + 4 * KB, BF16, [128, 2048])
    ych = cv(RA + 8 * KB, BF16, [128, 2048])
    stg = cv(RA + 12 * KB, BF16, [128, 1024])
    drepb = cv(RA + 14 * KB, BF16, [128, 2048])
    Gsb = cv(RA + 18 * KB, BF16, [128, 128])
    btok = cv(RA + 18 * KB + 512, BF16, [128, 128])
    eT = cv(RA + 19 * KB, F32, [128, 32])
    wTs = cv(RA + 19 * KB + 512, F32, [128, 32])
    stk64 = cv(RA + 60 * KB, F32, [64, 128])
    cdfm = cv(RA + 61 * KB, F32, [64, 16])
    ncl = cv(RA + 61 * KB + 512, F32, [64, 1])
    carry = cv(RA + 20 * KB, F32, [128, 2048])
    ybar = cv(RB, BF16, [128, 16, TOK])
    stkLs = cv(RB, F32, [3, 4096])
    stkRs = cv(RB + 16 * KB, F32, [3, 4096])
    Esb = cv(RB + 32 * KB, BF16, [128, 4, 1024])
    Msb = cv(RB + 40 * KB, BF16, [128, 4, 1024])
    xcT = cv(RB + 48 * KB, BF16, [128, TOK])
    xwt = cv(RB + 52 * KB, BF16, [128, 2048])
    prevT = cv(RB + 56 * KB, BF16, [128, 2048])
    ytile = cv(RB, BF16, [128, 2, 2048])
    zbuf = cv(RB + 8 * KB, BF16, [128, 2, 2048])
    sg2 = cv(RB + 16 * KB, BF16, [128, 2048])
    idf = cv(PERS, F32, [128, 128])
    idb = cv(PERS + 512, BF16, [128, 128])
    convw = cv(PERS + 1 * KB, F32, [128, 18, 4])
    convb = cv(PERS + 1 * KB + 512, F32, [128, 18])
    con = cv(PERS + 1 * KB + 640, F32, [128, 8])
    dtbc = cv(PERS + 1 * KB + 704, F32, [64, 1])
    Acol = cv(PERS + 1 * KB + 768, F32, [64, 1])
    orow = cv(PERS + 2 * KB, F32, [1, 128])
    ocolb = cv(PERS + 2 * KB + 512, BF16, [128, 1])
    cdrep = cv(PERS + 3 * KB, F32, [128, 16, 16])
    wex = cv(RA + 62 * KB, F32, [64, 128])

    PA = nc.alloc_psum_tensor("PA", [128, 2048], F32)
    PB = nc.alloc_psum_tensor("PB", [128, 2048], F32)
    PAb = PA.ap().bitcast(BF16)
    PBb = PB.ap().bitcast(BF16)

    s_ld = P.sem("s_ld")
    s_gd = P.sem("s_gd")
    s_t = P.sem("s_t")
    s_sc = P.sem("s_sc")
    s_v = P.sem("s_v")
    s_g = P.sem("s_g")
    s_sp = P.sem("s_sp")
    s_st = P.sem("s_st")

    g = dict(s_t=s_t, s_sc=s_sc, s_v=s_v, rows=rows, xsq=xsq, con=con,
             ones_colb=ocolb[:, 0:1], ones_row=orow[0:1, :], lnsc=lnsc,
             PBrow=PB.ap()[0:1, 0:1024], PBC=PB.ap()[:, 1024:2048])

    P.dma("sync", con[:, :], consts, s_ld)
    P.dma("sync", orow[:, :], onesrow, s_ld)
    P.dma("sync", ocolb[:, :], onescolb, s_ld)
    P.dma("sync", idf[:, :], ident, s_ld)
    P.dma("sync", idb[:, :], identb, s_ld)
    P.dma("sync", convw[:, :, :], conv_w_sl, s_ld)
    P.dma("sync", convb[:, :], conv_b_sl, s_ld)
    P.dma("sync", dtbc[:, :], dt_bias_c, s_ld)
    P.dma("sync", Acol[:, :], A_c, s_ld)
    P.dma("sync", stkL, stkL_init, s_ld)
    P.dma("sync", stkR, stkR_init, s_ld)
    for c in range(16):
        P.dma("gpsimd", xb[:, c, :], xT.rearrange("(c p) t -> c p t", p=128)[c], s_gd)
    ld0 = s_ld.n
    cast0 = s_gd.n
    for e in ("tensor", "scalar", "vector", "gpsimd"):
        P.wait(e, s_ld, ld0)
        P.wait(e, s_gd, cast0)

    emit_ln(P, g, lambda c: xb[:, c, :], 16,
            lambda c, t0, t1: ybar[:, c, t0:t1], tok=TOK)
    ybar_done = s_v.n

    wsrc = w_in_sl.rearrange("(c p) m -> c p m", p=128)
    tile_mm_done = {}
    evict_done = {}
    P.wait("vector", s_v, ybar_done)

    def load_wcol(i, mt, mwid=128):
        buf = i % 2
        if i < 2:
            P.wait("gpsimd", s_v, ybar_done)
        if i >= 2:
            P.wait("gpsimd", s_t, tile_mm_done[i - 2])
        P.dma("gpsimd", wcol[:, buf, :, 0:mwid],
              wsrc[:, :, mt * 128:mt * 128 + mwid].transpose([1, 0, 2]), s_gd)
        return s_gd.n, buf

    def emit_tile_mms(i, buf, wd, psum, mwid=128, mpart=128):
        P.wait("tensor", s_gd, wd)
        P.wait("tensor", s_v, ybar_done)
        if i >= 2 and (i - 2) in evict_done:
            P.wait("tensor", s_v, evict_done[i - 2])
        r = None
        for half in range(4):
            t0 = half * 512
            for k in range(16):
                last = (half == 3 and k == 15)
                r = P.op("tensor", lambda t, k=k, t0=t0, buf=buf, psum=psum, mwid=mwid, mpart=mpart:
                     t.matmul(psum[0:mpart, t0:t0 + 512], wcol[:, buf, k, 0:mwid],
                              ybar[:, k, t0:t0 + 512], start=(k == 0), stop=(k == 15)),
                     sem=(s_t if last else None))
        tile_mm_done[i] = r
        return r

    silu_prev = [0]

    def evict_conv(i, ti, psum):
        P.wait("vector", s_t, tile_mm_done[i])
        if silu_prev[0]:
            P.wait("vector", s_sc, silu_prev[0])
        ev = P.op("vector", lambda v, psum=psum: v.tensor_copy(xraw[:, :], psum[:, :]), sem=s_v)
        evict_done[i] = ev
        if AB_DEBUG and ti == 0:
            P.wait("sync", s_v, ev)
            P.dma("sync", dbg_xraw, xraw[:, :], s_st)
        P.op("vector", lambda v, ti=ti: v.tensor_scalar(
            craw[:, :], xraw[:, :], convw[:, ti, 3:4], convb[:, ti:ti + 1],
            ALU.mult, ALU.add), sem=None)
        cv = None
        for kk, sh in ((2, 1), (1, 2), (0, 3)):
            cv = P.op("vector", lambda v, ti=ti, kk=kk, sh=sh: v.scalar_tensor_tensor(
                craw[:, sh:TOK], xraw[:, 0:TOK - sh], convw[:, ti, kk:kk + 1],
                craw[:, sh:TOK], ALU.mult, ALU.add), sem=s_v)
        P.wait("scalar", s_v, cv)
        sl = P.op("scalar", lambda s, ti=ti: s.activation(
            xc[:, ti, :], craw[:, :], AF.Silu), sem=s_sc)
        silu_prev[0] = sl
        return sl

    i = 0
    for ti in range(18):
        mt = 16 + ti
        wd, buf = load_wcol(i, mt)
        psum = PA.ap() if (i % 2 == 0) else PB.ap()
        emit_tile_mms(i, buf, wd, psum)
        evict_conv(i, ti, psum)
        i += 1
    wd, buf = load_wcol(i, 34, mwid=64)
    psum_dt = PA.ap() if (i % 2 == 0) else PB.ap()
    emit_tile_mms(i, buf, wd, psum_dt, mwid=64, mpart=64)
    i += 1

    P.wait("scalar", s_t, tile_mm_done[i - 1])
    P.op("scalar", lambda s: s.activation(dt0[:, :], psum_dt[0:64, :], AF.Abs,
                                          bias=dtbc[:, 0:1]), sem=None)
    P.op("scalar", lambda s: s.activation(dt0[:, :], dt0[:, :], AF.Exp, scale=-1.0), sem=None)
    P.op("scalar", lambda s: s.activation(dt0[:, :], dt0[:, :], AF.Ln,
                                          bias=con[0:64, 0:1]), sem=None)
    P.op("scalar", lambda s: s.activation(dt1[:, :], psum_dt[0:64, :], AF.Relu,
                                          bias=dtbc[:, 0:1]), sem=s_sc)
    sp1 = s_sc.n
    P.wait("vector", s_sc, sp1)
    P.op("vector", lambda v: v.tensor_tensor(dt1[:, :], dt1[:, :], dt0[:, :], ALU.add), sem=s_v)
    dts_done = s_v.n
    P.wait("scalar", s_v, dts_done)
    P.op("scalar", lambda s: s.activation(dt2[:, :], dt1[:, :], AF.Ln), sem=s_sc)
    lndt_done = s_sc.n
    P.wait("vector", s_sc, lndt_done)
    P.op("vector", lambda v: v.tensor_scalar(dt0[:, :], dt1[:, :], Acol[:, 0:1], None,
                                             ALU.mult), sem=s_v)
    dA_done = s_v.n
    for c in range(16):
        P.op("vector", lambda v, c=c: v.tensor_tensor_scan(
            dt3[:, c * 128:(c + 1) * 128], dt0[:, c * 128:(c + 1) * 128],
            dt0[:, c * 128:(c + 1) * 128], 0.0, ALU.add, ALU.bypass), sem=None)
    P.op("vector", lambda v: v.tensor_single_scalar(dt0[:, :], dt3[:, :], -1.0,
                                                    ALU.mult), sem=s_v)
    ca_done = s_v.n
    P.wait("scalar", s_v, ca_done)
    P.op("scalar", lambda s: s.activation(
        cdfm[:, :], dt3.rearrange("p (c l) -> p c l", l=128)[:, :, 127], AF.Exp),
        sem=s_sc)
    cd_done = s_sc.n
    P.wait("sync", s_v, ca_done)
    P.wait("sync", s_sc, lndt_done)
    stko = stkL.rearrange("(c r) (h l) -> c r h l", r=3, l=128)
    stkoR = stkR.rearrange("(c r) (h l) -> c r h l", r=3, l=128)
    P.dma("sync", stko[:, 0].transpose([1, 0, 2]), dt0[0:32, :].rearrange("h (c l) -> h c l", l=128), s_sp)
    P.dma("sync", stko[:, 2].transpose([1, 0, 2]),
          dt2[0:32, :].rearrange("h (c l) -> h c l", l=128), s_sp)
    P.dma("sync", stkoR[:, 1].transpose([1, 0, 2]), dt3[0:32, :].rearrange("h (c l) -> h c l", l=128), s_sp)
    P.wait("sync", s_sc, cd_done)
    P.dma("sync", cd_dr, cdfm[0:32, :], s_sp)
    stk_done = s_sp.n
    P.wait("sync", s_sp, stk_done)
    P.dma("sync", cdrep[0:64, :, :],
          bass.AP(cd_dr.tensor, 0, [[0, 64], [32, 16], [1, 16]]), s_sp)
    P.dma("sync", cdrep[64:128, :, :],
          bass.AP(cd_dr.tensor, 16, [[0, 64], [32, 16], [1, 16]]), s_sp)
    cdrep_done = s_sp.n

    for c in range(16):
        P.dma("sync", ybar_sp[c], ybar[:, c, :], s_sp)
    spill_done = s_sp.n
    if AB_DEBUG:
        for c in range(16):
            P.dma("sync", dbg_ybar[c], ybar[:, c, :], s_st)
        for tii in range(18):
            P.dma("sync", dbg_xc[tii], xc[:, tii, :], s_st)
        P.wait("sync", s_v, dA_done)
        P.dma("sync", dbg_dt, dt1[:, :], s_st)
        P.wait("sync", s_v, ca_done)
        P.dma("sync", dbg_cA, dt3[:, :], s_st)

    P.wait("tensor", s_sp, spill_done)
    P.wait("vector", s_sp, spill_done)
    P.wait("scalar", s_sp, spill_done)
    P.wait("gpsimd", s_sp, spill_done)
    P.wait("tensor", s_sp, stk_done)
    P.wait("vector", s_sp, cdrep_done)
    P.op("vector", lambda v: v.memset(carry[:, :], 0.0), sem=None)
    P.op("vector", lambda v: v.memset(usb[:, :], 0.0), sem=s_v)
    init_done = s_v.n
    P.wait("sync", s_v, ybar_done)
    P.dma("sync", drow[:, :], D_rep, s_sp)
    drow_ld = s_sp.n
    P.wait("tensor", s_sp, drow_ld)
    P.wait("tensor", s_v, init_done)
    for q in range(4):
        P.op("tensor", lambda t, q=q: t.matmul(
            PA.ap()[:, q * 512:(q + 1) * 512], orow[0:1, :], drow[0:1, q * 512:(q + 1) * 512],
            start=True, stop=True), sem=(s_t if q == 3 else None))
    drep_mm = s_t.n
    P.wait("vector", s_t, drep_mm)
    P.op("vector", lambda v: v.tensor_copy(drepb[:, :], PA.ap()[:, :]), sem=s_v)
    drep_done = s_v.n
    P.wait("tensor", s_v, drep_done)

    sEdone = 0
    last_states = 0
    stg_dma = [0]
    for c in range(16):
        c0, c1 = c * 128, (c + 1) * 128
        if c > 0:
            P.wait("sync", s_t, s_t.n)
        P.dma("sync", stkLs[:, :], stkL[3 * c:3 * c + 3, :], s_sp)
        P.dma("sync", stkRs[:, :], stkR[3 * c:3 * c + 3, :], s_sp)
        stk_c = s_sp.n
        if c > 0:
            P.wait("tensor", s_v, last_rec)
        P.op("tensor", lambda t, c0=c0, c1=c1: t.matmul(
            PA.ap()[:, 0:128], xc[:, 16, c0:c1], xc[:, 17, c0:c1],
            start=True, stop=True), sem=s_t)
        gmm = s_t.n
        P.wait("vector", s_t, gmm)
        P.op("vector", lambda v: v.tensor_copy(Gsb[:, :], PA.ap()[:, 0:128]), sem=s_v)
        gev = s_v.n
        P.op("vector", lambda v, c=c: v.tensor_copy(
            ncl[:, 0:1], dt3.rearrange("p (c l) -> p c l", l=128)[:, c, 127:128]), sem=s_v)
        nclv = s_v.n
        P.wait("scalar", s_v, nclv)
        P.op("scalar", lambda s, c0=c0, c1=c1: s.activation(
            wex[:, :], dt3[:, c0:c1], AF.Exp, bias=ncl[:, 0:1], scale=-1.0), sem=s_sc)
        wexv = s_sc.n
        P.wait("vector", s_sc, wexv)
        P.op("vector", lambda v, c0=c0, c1=c1: v.tensor_tensor(
            wex[:, :], wex[:, :], dt1[:, c0:c1], ALU.mult), sem=None)
        P.op("vector", lambda v, c0=c0, c1=c1: v.tensor_copy(stk64[0:32, :], dt3[0:32, c0:c1]), sem=None)
        P.op("vector", lambda v: v.tensor_copy(stk64[32:64, :], wex[0:32, :]), sem=s_v)
        stk64v = s_v.n
        P.wait("tensor", s_v, stk64v)
        P.op("tensor", lambda t: t.transpose(PA.ap()[:, 1536:1600], stk64[:, :], idf[0:64, 0:64]),
             sem=s_t)
        trv = s_t.n
        P.wait("scalar", s_t, trv)
        P.op("scalar", lambda s: s.activation(eT[:, :], PA.ap()[:, 1536:1568], AF.Exp), sem=s_sc)
        eTv = s_sc.n
        P.wait("vector", s_t, trv)
        P.op("vector", lambda v: v.tensor_copy(wTs[:, :], PA.ap()[:, 1568:1600]), sem=s_v)
        wT_done = s_v.n
        P.wait("tensor", s_v, gev)
        P.wait("tensor", s_v, wT_done)
        xcT_done = 0
        for wv in range(2):
            for j in range(8):
                ti = wv * 8 + j
                P.op("tensor", lambda t, ti=ti, j=j, c0=c0, c1=c1: t.transpose(
                    PAb[:, 3072 + j * 128:3072 + (j + 1) * 128], xc[:, ti, c0:c1], idb[:, :]),
                    sem=s_t)
            trw = s_t.n
            P.wait("vector", s_t, trw)
            for j in range(8):
                ti = wv * 8 + j
                P.op("vector", lambda v, ti=ti, j=j: v.tensor_copy(
                    xcT[:, ti * 128:(ti + 1) * 128], PAb[:, 3072 + j * 128:3072 + (j + 1) * 128]),
                    sem=s_v)
            xcT_done = s_v.n
            P.wait("tensor", s_v, xcT_done)
        P.op("tensor", lambda t, c0=c0, c1=c1: t.transpose(
            PAb[:, 3072:3200], xc[:, 16, c0:c1], idb[:, :]), sem=s_t)
        btr = s_t.n
        P.wait("vector", s_t, btr)
        P.op("vector", lambda v: v.tensor_copy(btok[:, :], PAb[:, 3072:3200]), sem=s_v)
        btok_done = s_v.n
        P.op("vector", lambda v: v.tensor_tensor(
            xwt.rearrange("p (h q) -> p h q", q=64),
            xcT.rearrange("p (h q) -> p h q", q=64),
            wTs.unsqueeze(2).broadcast_to([128, 32, 64]), ALU.mult), sem=s_v)
        xw_done = s_v.n
        P.wait("tensor", s_sp, stk_c)
        for grp in range(4):
            dmm = None
            for j in range(8):
                h = grp * 8 + j
                dmm = P.op("tensor", lambda t, h=h, j=j: t.matmul(
                    PA.ap()[:, 512 + j * 128:512 + (j + 1) * 128],
                    stkLs[:, h * 128:(h + 1) * 128], stkRs[:, h * 128:(h + 1) * 128],
                    start=True, stop=True), sem=s_t)
            P.wait("scalar", s_t, dmm)
            if grp == 0:
                P.wait("scalar", s_sc, eTv)
            ex = P.op("scalar", lambda s, grp=grp: s.activation(
                Esb[:, grp, :], PA.ap()[:, 512:1536], AF.Exp), sem=s_sc)
            P.wait("gpsimd", s_sc, ex)
            mk = P.op("gpsimd", lambda gg, grp=grp: gg.affine_select(
                Esb.rearrange("p g (j l) -> p g j l", l=128)[:, grp],
                Esb.rearrange("p g (j l) -> p g j l", l=128)[:, grp],
                [[0, 8], [1, 128]], ALU.is_ge, 0.0, base=0, channel_multiplier=-1),
                sem=s_g)
            P.wait("vector", s_g, mk)
            P.wait("vector", s_v, gev)
            P.op("vector", lambda v, grp=grp: v.scalar_tensor_tensor(
                Msb.rearrange("p g (j l) -> p g j l", l=128)[:, grp],
                Esb.rearrange("p g (j l) -> p g j l", l=128)[:, grp], 1.0,
                Gsb.unsqueeze(1).broadcast_to([128, 8, 128]), ALU.mult, ALU.mult),
                sem=s_v)
            mdone = s_v.n
            P.wait("tensor", s_v, mdone)
        m_all = s_v.n
        if c == 0:
            P.op("vector", lambda v: v.memset(prevT[:, :], 0.0), sem=s_v)
            prevT_done = s_v.n
        else:
            P.wait("tensor", s_v, last_rec)
            for wv in range(4):
                for j in range(4):
                    jj = wv * 4 + j
                    P.op("tensor", lambda t, jj=jj, j=j: t.transpose(
                        PA.ap()[:, 1536 + j * 128:1536 + (j + 1) * 128],
                        carry[:, jj * 128:(jj + 1) * 128], idf[:, :]), sem=s_t)
                trw = s_t.n
                P.wait("vector", s_t, trw)
                for j in range(4):
                    jj = wv * 4 + j
                    P.op("vector", lambda v, jj=jj, j=j: v.tensor_copy(
                        prevT[:, jj * 128:(jj + 1) * 128],
                        PA.ap()[:, 1536 + j * 128:1536 + (j + 1) * 128]), sem=s_v)
                P.wait("tensor", s_v, s_v.n)
            prevT_done = s_v.n
        P.wait("tensor", s_v, m_all)
        P.wait("tensor", s_v, xcT_done)
        yintra = None
        for h in range(32):
            yintra = P.op("tensor", lambda t, h=h: t.matmul(
                PB.ap()[:, h * 64:(h + 1) * 64],
                Msb[:, h // 8, (h % 8) * 128:(h % 8) * 128 + 128],
                xcT[:, h * 64:(h + 1) * 64], start=True, stop=True), sem=s_t)
        if c > 0:
            P.wait("tensor", s_v, prevT_done)
            for half in range(2):
                o = half * 1024
                P.op("tensor", lambda t, o=o, c0=c0, c1=c1: t.matmul(
                    PA.ap()[:, 512:1024], xc[:, 17, c0:c1], prevT[:, o:o + 512],
                    start=True, stop=True), sem=None)
                P.op("tensor", lambda t, o=o, c0=c0, c1=c1: t.matmul(
                    PA.ap()[:, 1024:1536], xc[:, 17, c0:c1], prevT[:, o + 512:o + 1024],
                    start=True, stop=True), sem=s_t)
                yi = s_t.n
                P.wait("vector", s_t, yi)
                P.wait("vector", s_sc, eTv)
                P.op("vector", lambda v, o=o, half=half: v.scalar_tensor_tensor(
                    usb.rearrange("p (h q) -> p h q", q=64)[:, half * 16:half * 16 + 16],
                    PA.ap()[:, 512:1536].rearrange("p (h q) -> p h q", q=64),
                    1.0, eT[:, half * 16:half * 16 + 16].unsqueeze(2).broadcast_to([128, 16, 64]),
                    ALU.mult, ALU.mult), sem=s_v)
                P.wait("tensor", s_v, s_v.n)
        u_done = s_v.n
        P.op("vector", lambda v: v.tensor_tensor(vsb[:, :], xcT[:, :], drepb[:, :], ALU.mult), sem=None)
        P.op("vector", lambda v: v.tensor_tensor(vsb[:, :], vsb[:, :], usb[:, :], ALU.add), sem=s_v)
        P.wait("vector", s_t, yintra)
        P.op("vector", lambda v: v.scalar_tensor_tensor(
            ych[:, :], PB.ap()[:, :], 1.0, vsb[:, :], ALU.mult, ALU.add), sem=s_v)
        ych_done = s_v.n
        if AB_DEBUG and c < 2:
            P.wait("sync", s_v, ych_done)
            P.dma("sync", dbg_ych[c], ych[:, :], s_st)
            if c == 0:
                P.dma("sync", dbg_xcT, xcT[:, :], s_st)
                P.dma("sync", dbg_vsb, vsb[:, :], s_st)
                P.dma("sync", dbg_drepb, drepb[:, :], s_st)
            if c == 0:
                P.wait("sync", s_v, m_all)
                P.dma("sync", dbg_M.rearrange("p (g f) -> p g f", g=4), Msb[:, :, :], s_st)
        P.wait("tensor", s_v, xw_done)
        P.wait("tensor", s_v, btok_done)
        P.wait("tensor", s_v, u_done)
        st_mm = None
        for j in range(16):
            st_mm = P.op("tensor", lambda t, j=j: t.matmul(
                PA.ap()[:, j * 128:(j + 1) * 128], xwt[:, j * 128:(j + 1) * 128],
                btok[:, :], start=True, stop=True), sem=s_t)
        P.wait("vector", s_t, st_mm)
        last_rec = None
        for j in range(16):
            last_rec = P.op("vector", lambda v, j=j, c=c: v.scalar_tensor_tensor(
                carry[:, j * 128:(j + 1) * 128], carry[:, j * 128:(j + 1) * 128],
                cdrep[:, j, c:c + 1], PA.ap()[:, j * 128:(j + 1) * 128],
                ALU.mult, ALU.add), sem=s_v)
        P.wait("tensor", s_v, ych_done)
        for wv in range(2):
            for j in range(8):
                ti = wv * 8 + j
                P.op("tensor", lambda t, ti=ti, j=j: t.transpose(
                    PBb[:, j * 128:(j + 1) * 128], ych[:, ti * 128:(ti + 1) * 128],
                    idb[:, :]), sem=s_t)
            trw = s_t.n
            P.wait("vector", s_t, trw)
            P.wait("vector", s_sp, stg_dma[0])
            for j in range(8):
                P.op("vector", lambda v, j=j: v.tensor_copy(
                    stg[:, j * 128:(j + 1) * 128], PBb[:, j * 128:(j + 1) * 128]), sem=s_v)
            stgv = s_v.n
            P.wait("sync", s_v, stgv)
            P.dma("sync", bass.AP(y_dr.tensor, (wv * 8) * 128 * TOK + c0,
                                  [[2048, 128], [128 * TOK, 8], [1, 128]]),
                  stg.rearrange("p (ti l) -> p ti l", l=128), s_sp)
            stg_dma[0] = s_sp.n
            P.wait("tensor", s_v, stgv)
    ssd_done = s_sp.n
    P.wait("sync", s_sp, ssd_done)

    if AB_DEBUG:
        for tii in range(16):
            P.dma("sync", dbg_ydr[tii], y_dr[tii], s_st)
    for c in range(16):
        P.dma("sync", xc[:, c, :], ybar_sp[c], s_sp)
    yb2 = s_sp.n
    P.wait("tensor", s_sp, yb2)
    P.wait("gpsimd", s_sp, yb2)

    ztile_done = {}
    silu_done_p2 = {}
    for mt in range(16):
        buf = mt % 2
        if mt >= 2:
            P.wait("gpsimd", s_t, tile_mm_done[100 + mt - 2])
        P.dma("gpsimd", wcol[:, buf, :, :],
              wsrc[:, :, mt * 128:(mt + 1) * 128].transpose([1, 0, 2]), s_gd)
        wd = s_gd.n
        if mt >= 2:
            P.wait("sync", s_v, ztile_done[mt - 2])
        P.dma("sync", ytile[:, buf, :], y_dr[mt], s_sp)
        yld = s_sp.n
        psum = PA.ap() if (mt % 2 == 0) else PB.ap()
        P.wait("tensor", s_gd, wd)
        if mt >= 2:
            P.wait("tensor", s_sc, silu_done_p2[mt - 2])
        r = None
        for half in range(4):
            t0 = half * 512
            for k in range(16):
                last = (half == 3 and k == 15)
                r = P.op("tensor", lambda t, k=k, t0=t0, buf=buf, psum=psum: t.matmul(
                    psum[:, t0:t0 + 512], wcol[:, buf, k, :], xc[:, k, t0:t0 + 512],
                    start=(k == 0), stop=(k == 15)), sem=(s_t if last else None))
        tile_mm_done[100 + mt] = r
        P.wait("scalar", s_t, r)
        sl = P.op("scalar", lambda s, psum=psum: s.activation(
            sg2[:, :], psum[:, :], AF.Silu), sem=s_sc)
        silu_done_p2[mt] = sl
        P.wait("vector", s_sc, sl)
        P.wait("vector", s_sp, yld)
        P.op("vector", lambda v, buf=buf: v.tensor_tensor(
            zbuf[:, buf, :], sg2[:, :], ytile[:, buf, :], ALU.mult), sem=s_v)
        zv = s_v.n
        ztile_done[mt] = zv
        P.wait("sync", s_v, zv)
        P.dma("sync", z_out[mt], zbuf[:, buf, :], s_st)
        P.wait("scalar", s_v, zv)
    P.wait("sync", s_st, s_st.n)

    P.build()
    return nc


def host_AB_inputs(inputs, core):
    b, H = core // 2, core % 2
    x = inputs["L0_attn_hidden_states"][b]
    W = inputs["L0_attn_W_in"]
    gate_cols = np.arange(2048 * H, 2048 * H + 2048)
    x_cols = 4096 + np.arange(2048 * H, 2048 * H + 2048)
    bc_cols = np.arange(8192, 8448)
    myh = np.arange(32 * H, 32 * H + 32)
    oth = np.array([h for h in range(64) if h not in set(myh)])
    dt_cols = 8448 + np.concatenate([myh, oth])
    cols = np.concatenate([gate_cols, x_cols, bc_cols, dt_cols])
    w_in_sl = f32c(W[:, cols])

    conv_rows = np.concatenate([2048 * H + np.arange(2048), 4096 + np.arange(256)])
    cw = inputs["L0_attn_conv_w"][conv_rows]
    cb = inputs["L0_attn_conv_b"][conv_rows]
    conv_w_sl = f32c(cw.reshape(18, 128, 4).transpose(1, 0, 2))
    conv_b_sl = f32c(cb.reshape(18, 128).T)

    hord = np.concatenate([myh, oth])
    dt_bias_c = f32c(inputs["L0_attn_dt_bias"][hord][:, None])
    A_c = f32c(inputs["L0_attn_A"][hord][:, None])
    D_rep = f32c(np.broadcast_to(np.repeat(inputs["L0_attn_D"][myh], 64)[None, :], (128, 2048)))

    consts = np.zeros((128, 8), np.float32)
    consts[:, 0] = 1.0
    consts[:, 1] = EPS
    consts[:, 2] = EPS32
    stk_init = np.zeros((48, 4096), np.float32)
    stk_init[1::3] = 1.0
    stkR_init = np.zeros((48, 4096), np.float32)
    stkR_init[0::3] = 1.0
    stkR_init[2::3] = 1.0
    return {
        "xT": f32c(x.T),
        "w_in_sl": w_in_sl,
        "conv_w_sl": conv_w_sl,
        "conv_b_sl": conv_b_sl,
        "dt_bias_c": dt_bias_c,
        "A_c": A_c,
        "D_rep": D_rep,
        "ident": np.eye(128, dtype=np.float32),
        "identb": np.eye(128, dtype=np.float32).astype(BFD),
        "consts": consts,
        "onesrow": np.ones((1, 128), np.float32),
        "onescolb": np.ones((128, 1), np.float32).astype(BFD),
        "stkL_init": stk_init,
        "stkR_init": stkR_init,
    }


def run_AB(inputs):
    nc = build_AB()
    ins = [host_AB_inputs(inputs, c) for c in range(8)]
    res = run_bass_kernel_spmd(nc, ins, core_ids=list(range(8)))
    z = np.zeros((B, INTER, S), np.float32)
    for c in range(8):
        b, H = c // 2, c % 2
        zc = np.asarray(res.results[c]["z_out"]).astype(np.float32)
        z[b, 2048 * H:2048 * H + 2048] = zc.reshape(2048, 2048)
    return z


def _as_bf(a):
    a = np.asarray(a)
    if a.dtype != BFD:
        a = a.view(BFD)
    return a


def kernel(**inputs):
    cores = list(range(8))

    ncAB = build_AB()
    resAB = run_bass_kernel_spmd(ncAB, [host_AB_inputs(inputs, c) for c in cores],
                                 core_ids=cores)
    z_full = np.zeros((B, INTER, S), dtype=BFD)
    for c in cores:
        b, H = c // 2, c % 2
        zc = _as_bf(resAB.results[c]["z_out"]).reshape(2048, S)
        z_full[b, 2048 * H:2048 * H + 2048] = zc

    ncC = build_C()
    resC = run_bass_kernel_spmd(ncC, [host_C_inputs(inputs, c, z_full) for c in cores],
                                core_ids=cores)
    x2_full = np.zeros((B, DM, S), np.float32)
    yb2_full = np.zeros((B, DM, S), dtype=BFD)
    for c in cores:
        b, hh = c // 2, c % 2
        t0 = hh * 1024
        x2_full[b, :, t0:t0 + 1024] = np.asarray(resC.results[c]["x2_out"]).reshape(DM, 1024)
        yb2_full[b, :, t0:t0 + 1024] = _as_bf(resC.results[c]["yb2_out"]).reshape(DM, 1024)

    ncD = build_D()
    hinsD = []
    for c in cores:
        h = host_D_inputs(inputs, c)
        h["yb2_b"] = np.ascontiguousarray(yb2_full[c // 2].reshape(16, 128, S))
        hinsD.append(h)
    resD = run_bass_kernel_spmd(ncD, hinsD, core_ids=cores)
    oT_full = np.zeros((B, DM, S), dtype=BFD)
    wqgu = np.zeros((DM, 16384), dtype=BFD)
    wqp = np.zeros((8192, DM), dtype=BFD)
    for c in cores:
        b, H = c // 2, c % 2
        oT = _as_bf(resD.results[c]["oT_out"]).reshape(1024, S)
        oT_full[b, 1024 * H:1024 * H + 1024] = oT
        wqgu[:, 2048 * c:2048 * c + 2048] = _as_bf(resD.results[c]["wqgu_o"]).reshape(2048, 2048)
        sl = _as_bf(resD.results[c]["wqp_o"]).reshape(8, 128, 8, 256)
        wqp[:, 256 * c:256 * c + 256] = np.ascontiguousarray(sl.transpose(0, 2, 1, 3)).reshape(8192, 256)

    ncE = build_E()
    wqgu_t = np.ascontiguousarray(wqgu.reshape(16, 128, 16384))
    wqp_t = np.ascontiguousarray(wqp.reshape(64, 128, 2048))
    resE = run_bass_kernel_spmd(
        ncE, [host_E_inputs(inputs, c, oT_full, x2_full, wqgu_t, wqp_t) for c in cores],
        core_ids=cores)
    out = np.zeros((B, S, DM), np.float32)
    for c in cores:
        b, hh = c // 2, c % 2
        t0 = hh * 1024
        out[b, t0:t0 + 1024, :] = np.asarray(resE.results[c]["x4_out"]).reshape(DM, 1024).T
    return out



def build_C():
    nc = bass.Bass("TRN2", target_bir_lowering=False)
    P = Plan(nc)
    TOK = 1024

    z_sl = nc.dram_tensor("z_sl", [32, 128, TOK], BF16, kind="ExternalInput").ap()
    x_sl = nc.dram_tensor("x_sl", [16, 128, TOK], F32, kind="ExternalInput").ap()
    w_out = nc.dram_tensor("w_out", [INTER, DM], F32, kind="ExternalInput").ap()
    w0T = nc.dram_tensor("w0T", [DM, DM], F32, kind="ExternalInput").ap()
    nw_c = nc.dram_tensor("nw_c", [128, 32], F32, kind="ExternalInput").ap()
    gam_c = nc.dram_tensor("gam_c", [128, 16], F32, kind="ExternalInput").ap()
    bet_c = nc.dram_tensor("bet_c", [128, 16], F32, kind="ExternalInput").ap()
    lnw_c = nc.dram_tensor("lnw_c", [128, 16], F32, kind="ExternalInput").ap()
    lnb_c = nc.dram_tensor("lnb_c", [128, 16], F32, kind="ExternalInput").ap()
    consts = nc.dram_tensor("consts", [128, 8], F32, kind="ExternalInput").ap()
    onesrow = nc.dram_tensor("onesrow", [1, 128], F32, kind="ExternalInput").ap()
    onescolb = nc.dram_tensor("onescolb", [128, 1], BF16, kind="ExternalInput").ap()
    ident = nc.dram_tensor("ident", [128, 128], F32, kind="ExternalInput").ap()

    x2_out = nc.dram_tensor("x2_out", [16, 128, TOK], F32, kind="ExternalOutput").ap()
    yb2_out = nc.dram_tensor("yb2_out", [16, 128, TOK], BF16, kind="ExternalOutput").ap()

    scr = nc.dram_tensor("scr", [1, 128], F32, kind="Internal").ap()

    rem = nc.sbuf_bytes_remaining
    arena_h = nc.alloc_sbuf_tensor("arena", [128, ((rem - 1024) // 512) * 512], mybir.dt.uint8)
    ar_f32 = arena_h.ap().bitcast(F32)
    ar_bf = arena_h.ap().bitcast(BF16)

    def cv(off, dtype, shape):
        parts = shape[0]
        elems = 1
        for d in shape[1:]:
            elems *= d
        base = ar_f32 if dtype == F32 else ar_bf
        isz = 4 if dtype == F32 else 2
        v = base[0:parts, off // isz: off // isz + elems]
        if len(shape) == 2:
            return v
        if len(shape) == 3:
            return v.rearrange("p (a b) -> p a b", a=shape[1], b=shape[2])
        return v.rearrange("p (a b c) -> p a b c", a=shape[1], b=shape[2], c=shape[3])

    KB = 1024
    R1 = 0
    zt = cv(R1, BF16, [128, 32, TOK])
    xsl = cv(R1, F32, [128, 16, TOK])
    xn = cv(R1, BF16, [128, 16, TOK])
    w0f = cv(R1 + 32 * KB, F32, [128, 2, 16, 128])
    wsgn = cv(R1 + 48 * KB, BF16, [128, 2, 16, 128])
    R2 = 64 * KB
    a0r = cv(R2, BF16, [128, 16, TOK])
    f0r = cv(R2, BF16, [128, 16, TOK])
    yb2 = cv(R2, BF16, [128, 16, TOK])
    R3 = 96 * KB
    x1 = cv(R3, F32, [128, 16, TOK])
    R4 = 160 * KB
    x1b = cv(R4, BF16, [128, 16, TOK])
    x2b = cv(R4, BF16, [128, 16, TOK])
    wcol = cv(160 * KB, BF16, [128, 2, 32, 128])
    PR = 192 * KB
    xsq = cv(PR, BF16, [128, 1024])
    lnsc = cv(PR + 2 * KB, F32, [128, 512])
    rows = cv(PR + 4 * KB, F32, [1, 3, 512])
    maxc = cv(PR + 10 * KB, F32, [128, 18])
    con = cv(PR + 10 * KB + 512, F32, [128, 8])
    orow = cv(PR + 11 * KB, F32, [1, 128])
    ocolb = cv(PR + 11 * KB + 512, BF16, [128, 1])
    idf = cv(PR + 12 * KB, F32, [128, 128])
    nwc = cv(PR + 12 * KB + 512, F32, [128, 32])
    gac = cv(PR + 13 * KB, F32, [128, 16])
    bec = cv(PR + 13 * KB + 128, F32, [128, 16])
    lwc = cv(PR + 13 * KB + 256, F32, [128, 16])
    lbc = cv(PR + 13 * KB + 384, F32, [128, 16])
    g2c = cv(PR + 13 * KB + 512, F32, [128, 16])
    srow = cv(PR + 14 * KB, F32, [1, 128])
    scol = cv(PR + 13 * KB + 640, F32, [128, 1])

    PA = nc.alloc_psum_tensor("PA", [128, 2048], F32)
    PB = nc.alloc_psum_tensor("PB", [128, 2048], F32)

    s_ld = P.sem("s_ld")
    s_gd = P.sem("s_gd")
    s_t = P.sem("s_t")
    s_sc = P.sem("s_sc")
    s_v = P.sem("s_v")
    s_sp = P.sem("s_sp")
    s_st = P.sem("s_st")

    g = dict(s_t=s_t, s_sc=s_sc, s_v=s_v, rows=rows, xsq=xsq, con=con,
             ones_colb=ocolb[:, 0:1], ones_row=orow[0:1, :], lnsc=lnsc,
             PBrow=PB.ap()[0:1, 0:1024], PBC=PB.ap()[:, 1024:2048])

    for nm, dst, srcap in [("con", con, consts), ("orow", orow, onesrow),
                           ("ocolb", ocolb, onescolb), ("idf", idf, ident),
                           ("nwc", nwc, nw_c), ("gac", gac, gam_c), ("bec", bec, bet_c),
                           ("lwc", lwc, lnw_c), ("lbc", lbc, lnb_c)]:
        P.dma("sync", dst[:, :], srcap, s_ld)
    for c in range(32):
        P.dma("sync", zt[:, c, :], z_sl[c], s_ld)
    ld0 = s_ld.n
    for e in ("tensor", "scalar", "vector", "gpsimd"):
        P.wait(e, s_ld, ld0)

    inv = 1.0 / INTER
    for w in range(2):
        t0 = w * 512
        P.wait("tensor", s_v, s_v.n)
        last_mm = [None]
        for c in range(32):
            sq = P.op("scalar", lambda s, c=c, t0=t0: s.activation(
                xsq[:, (c % 2) * 512:(c % 2) * 512 + 512], zt[:, c, t0:t0 + 512], AF.Square),
                sem=s_sc)
            P.wait("tensor", s_sc, sq)
            mm = P.op("tensor", lambda t, c=c: t.matmul(
                PB.ap()[0:1, 0:512], ocolb[:, 0:1],
                xsq[:, (c % 2) * 512:(c % 2) * 512 + 512],
                start=(c == 0), stop=(c == 31)), sem=s_t)
            if last_mm[0] is not None:
                P.wait("scalar", s_t, last_mm[0])
            last_mm[0] = mm
        P.wait("vector", s_t, s_t.n)
        P.op("vector", lambda v: v.tensor_single_scalar(
            rows[0:1, 0, :], PB.ap()[0:1, 0:512], inv, ALU.mult), sem=s_v)
        P.wait("scalar", s_v, s_v.n)
        P.op("scalar", lambda s: s.activation(
            rows[0:1, 1, :], rows[0:1, 0, :], AF.Ln, bias=con[0:1, 1:2]), sem=None)
        P.op("scalar", lambda s: s.activation(
            rows[0:1, 0, :], rows[0:1, 1, :], AF.Exp, scale=-0.5), sem=s_sc)
        P.wait("tensor", s_sc, s_sc.n)
        bcv = P.op("tensor", lambda t: t.matmul(
            PB.ap()[:, 1024:1536], orow[0:1, :], rows[0:1, 0, :], start=True, stop=True), sem=s_t)
        P.wait("vector", s_t, bcv)
        for c in range(32):
            P.op("vector", lambda v, c=c, t0=t0: v.scalar_tensor_tensor(
                zt[:, c, t0:t0 + 512], zt[:, c, t0:t0 + 512], nwc[:, c:c + 1],
                PB.ap()[:, 1024:1536], ALU.mult, ALU.mult), sem=s_v)
    zn_done = s_v.n

    wsrc = w_out.rearrange("(c p) m -> c p m", p=128)
    P.wait("gpsimd", s_v, zn_done)
    mmdone = {}
    evdone = {}
    for mt in range(16):
        buf = mt % 2
        if mt >= 2:
            P.wait("gpsimd", s_t, mmdone[mt - 2])
        P.dma("gpsimd", wcol[:, buf, :, :],
              wsrc[:, :, mt * 128:(mt + 1) * 128].transpose([1, 0, 2]), s_gd)
        wd = s_gd.n
        psum = PA.ap() if buf == 0 else PB.ap()
        P.wait("tensor", s_gd, wd)
        P.wait("tensor", s_v, zn_done)
        if mt >= 2:
            P.wait("tensor", s_v, evdone[mt - 2])
        r = None
        for w in range(2):
            t0 = w * 512
            for k in range(32):
                r = P.op("tensor", lambda t, k=k, t0=t0, buf=buf, psum=psum: t.matmul(
                    psum[:, t0:t0 + 512], wcol[:, buf, k, :], zt[:, k, t0:t0 + 512],
                    start=(k == 0), stop=(k == 31)), sem=(s_t if (w == 1 and k == 31) else None))
        mmdone[mt] = r
        P.wait("vector", s_t, r)
        ev = P.op("vector", lambda v, mt=mt, psum=psum: v.tensor_copy(
            a0r[:, mt, :], psum[:, 0:1024]), sem=s_v)
        evdone[mt] = ev
    a0_done = s_v.n

    P.wait("sync", s_v, a0_done)
    for c in range(16):
        P.dma("sync", xsl[:, c, :], x_sl[c], s_sp)
    xld = s_sp.n
    P.wait("vector", s_sp, xld)
    P.wait("scalar", s_v, a0_done)
    emit_ln(P, g, lambda c: a0r[:, c, :], 16,
            None, tok=TOK,
            res_src=lambda c, t0, t1: xsl[:, c, t0:t1],
            res_dst=lambda c, t0, t1: x1[:, c, t0:t1],
            res_bf=lambda c, t0, t1: x1b[:, c, t0:t1])
    x1_done = s_v.n

    P.wait("scalar", s_v, x1_done)
    emit_ln(P, g, lambda c: x1b[:, c, :], 16,
            lambda c, t0, t1: xn[:, c, t0:t1], tok=TOK,
            w_col=lambda c: lwc[:, c:c + 1], b_col=lambda c: lbc[:, c:c + 1])
    xn_done = s_v.n

    w0src = w0T.rearrange("(c p) m -> c p m", p=128)
    P.wait("gpsimd", s_v, x1_done)
    mmdone2 = {}
    sgndone = {}
    for mt in range(16):
        buf = mt % 2
        if mt >= 2:
            P.wait("gpsimd", s_t, mmdone2[mt - 2])
        P.dma("gpsimd", w0f[:, buf, :, :],
              w0src[:, :, mt * 128:(mt + 1) * 128].transpose([1, 0, 2]), s_gd)
        wd = s_gd.n
        P.wait("scalar", s_gd, wd)
        if mt >= 2:
            P.wait("scalar", s_t, mmdone2[mt - 2])
        sg = P.op("scalar", lambda s, buf=buf: s.activation(
            wsgn[:, buf, :, :], w0f[:, buf, :, :], AF.Sign), sem=s_sc)
        sgndone[mt] = sg
        P.wait("vector", s_gd, wd)
        P.op("vector", lambda v, mt=mt, buf=buf: v.reduce_max(
            maxc[:, mt + 1:mt + 2], w0f[:, buf, :, :].rearrange("p a b -> p (a b)"),
            axis=AX.X, apply_absolute_value=True), sem=s_v)
        psum = PA.ap() if buf == 0 else PB.ap()
        P.wait("tensor", s_sc, sg)
        P.wait("tensor", s_v, xn_done)
        if mt >= 2:
            P.wait("tensor", s_v, evdone[100 + mt - 2])
        r = None
        for w in range(2):
            t0 = w * 512
            for k in range(16):
                r = P.op("tensor", lambda t, k=k, t0=t0, buf=buf, psum=psum: t.matmul(
                    psum[:, t0:t0 + 512], wsgn[:, buf, k, :], xn[:, k, t0:t0 + 512],
                    start=(k == 0), stop=(k == 15)), sem=(s_t if (w == 1 and k == 15) else None))
        mmdone2[mt] = r
        P.wait("vector", s_t, r)
        ev = P.op("vector", lambda v, mt=mt, psum=psum: v.tensor_copy(
            f0r[:, mt, :], psum[:, 0:1024]), sem=s_v)
        evdone[100 + mt] = ev
    f0raw_done = s_v.n

    P.op("vector", lambda v: v.reduce_max(
        maxc[:, 0:1], maxc[:, 1:17], axis=AX.X), sem=s_v)
    mx1 = s_v.n
    P.wait("tensor", s_v, mx1)
    P.op("tensor", lambda t: t.transpose(PA.ap()[0:1, 0:128], maxc[:, 0:1], idf[:, :]), sem=s_t)
    trm = s_t.n
    P.wait("vector", s_t, trm)
    P.op("vector", lambda v: v.tensor_copy(srow[0:1, :], PA.ap()[0:1, 0:128]), sem=s_v)
    P.op("vector", lambda v: v.reduce_max(srow[0:1, 0:1], srow[0:1, :], axis=AX.X), sem=s_v)
    sc1 = s_v.n
    P.wait("sync", s_v, sc1)
    P.dma("sync", scr, srow[0:1, :], s_sp)
    P.wait("sync", s_sp, s_sp.n)
    P.dma("sync", scol[:, 0:1], bass.AP(scr.tensor, 0, [[0, 128], [1, 1]]), s_sp)
    scl = s_sp.n
    P.wait("vector", s_sp, scl)
    P.op("vector", lambda v: v.tensor_scalar(g2c[:, :], gac[:, :], scol[:, 0:1], None,
                                             ALU.mult), sem=s_v)
    g2_done = s_v.n

    P.wait("vector", s_v, g2_done)
    P.wait("vector", s_v, f0raw_done)
    for c in range(16):
        P.op("vector", lambda v, c=c: v.tensor_scalar(
            f0r[:, c, :], f0r[:, c, :], g2c[:, c:c + 1], bec[:, c:c + 1],
            ALU.mult, ALU.add), sem=s_v)
    f0_done = s_v.n

    P.wait("scalar", s_v, f0_done)
    emit_ln(P, g, lambda c: f0r[:, c, :], 16,
            None, tok=TOK,
            res_src=lambda c, t0, t1: x1[:, c, t0:t1],
            res_dst=lambda c, t0, t1: x1[:, c, t0:t1],
            res_bf=lambda c, t0, t1: x2b[:, c, t0:t1])
    x2_done = s_v.n
    P.wait("sync", s_v, x2_done)
    for c in range(16):
        P.dma("sync", x2_out[c], x1[:, c, :], s_st)
    P.wait("scalar", s_v, x2_done)
    emit_ln(P, g, lambda c: x2b[:, c, :], 16,
            lambda c, t0, t1: yb2[:, c, t0:t1], tok=TOK)
    yb2_done = s_v.n
    P.wait("sync", s_v, yb2_done)
    for c in range(16):
        P.dma("sync", yb2_out[c], yb2[:, c, :], s_st)
    P.wait("sync", s_st, s_st.n)

    P.build()
    return nc


def host_C_inputs(inputs, core, z_full):
    b, hh = core // 2, core % 2
    t0 = hh * 1024
    xT = np.ascontiguousarray(inputs["L0_attn_hidden_states"][b].T)
    consts = np.zeros((128, 8), np.float32)
    consts[:, 0] = 1.0
    consts[:, 1] = EPS
    consts[:, 2] = EPS32

    def colmajor(vec, nch):
        return f32c(np.asarray(vec).reshape(nch, 128).T)

    return {
        "z_sl": np.ascontiguousarray(z_full[b].reshape(32, 128, S)[:, :, t0:t0 + 1024]),
        "x_sl": f32c(xT.reshape(16, 128, S)[:, :, t0:t0 + 1024]),
        "w_out": f32c(inputs["L0_attn_W_out"]),
        "w0T": f32c(inputs["L0_ffn_weight"].T),
        "nw_c": colmajor(inputs["L0_attn_norm_weight"], 32),
        "gam_c": colmajor(inputs["L0_ffn_gamma"], 16),
        "bet_c": colmajor(inputs["L0_ffn_beta"], 16),
        "lnw_c": colmajor(inputs["L0_ffn_ln_weight"], 16),
        "lnb_c": colmajor(inputs["L0_ffn_ln_bias"], 16),
        "consts": consts,
        "onesrow": np.ones((1, 128), np.float32),
        "onescolb": np.ones((128, 1), np.float32).astype(BFD),
        "ident": np.eye(128, dtype=np.float32),
    }



def build_D():
    nc = bass.Bass("TRN2", target_bir_lowering=False)
    P = Plan(nc)
    TOK = S

    yb2_b = nc.dram_tensor("yb2_b", [16, 128, TOK], BF16, kind="ExternalInput").ap()
    wq_sl = nc.dram_tensor("wq_sl", [DM, 1024], F32, kind="ExternalInput").ap()
    wkv_sl = nc.dram_tensor("wkv_sl", [DM, 1024], F32, kind="ExternalInput").ap()
    saL_i = nc.dram_tensor("saL_i", [2, 1024], F32, kind="ExternalInput").ap()
    saR_i = nc.dram_tensor("saR_i", [2, 512], F32, kind="ExternalInput").ap()
    cb_i = nc.dram_tensor("cb_i", [128, 512], F32, kind="ExternalInput").ap()
    wguT_sl = nc.dram_tensor("wguT_sl", [16, 128, 2048], F32, kind="ExternalInput").ap()
    wpT_sl = nc.dram_tensor("wpT_sl", [8, 128, 2048], F32, kind="ExternalInput").ap()
    gsel_i = nc.dram_tensor("gsel_i", [128, 2], BF16, kind="ExternalInput").ap()
    consts = nc.dram_tensor("consts", [128, 8], F32, kind="ExternalInput").ap()
    onesrow = nc.dram_tensor("onesrow", [1, 128], F32, kind="ExternalInput").ap()
    onescolb = nc.dram_tensor("onescolb", [128, 1], BF16, kind="ExternalInput").ap()
    identb = nc.dram_tensor("identb", [128, 128], BF16, kind="ExternalInput").ap()

    oT_out = nc.dram_tensor("oT_out", [8, 128, TOK], BF16, kind="ExternalOutput").ap()
    wqgu_o = nc.dram_tensor("wqgu_o", [16, 128, 2048], BF16, kind="ExternalOutput").ap()
    wqp_o = nc.dram_tensor("wqp_o", [8, 128, 2048], BF16, kind="ExternalOutput").ap()

    scr_q = nc.dram_tensor("scr_q", [4, 2048], F32, kind="Internal").ap()
    scr_qs = nc.dram_tensor("scr_qs", [2, 2048], BF16, kind="Internal").ap()

    A = nc.alloc_sbuf_tensor
    yb2 = A("yb2", [128, 16, TOK], BF16)
    qT = A("qT", [128, 8, TOK], BF16)
    kT = A("kT", [128, 4, TOK], BF16)
    vT = A("vT", [128, 4, TOK], BF16)
    vtok = vT.ap().rearrange("p a (c q) -> p (a c) q", q=128)
    ptile = A("ptile", [128, 2, 512], BF16)
    rbv = A("rbv", [128, 512], F32)
    rrow = A("rrow", [1, 512], F32)
    obuf = A("obuf", [128, 512], BF16)
    wcolq = A("wcolq", [128, 2, 16, 128], BF16)
    saL = A("saL", [2, 1024], F32)
    saR = A("saR", [2, 512], F32)
    cbt = A("cbt", [128, 512], F32)
    con = A("con", [128, 8], F32)
    orow = A("orow", [1, 128], F32)
    ocolb = A("ocolb", [128, 1], BF16)
    idb = A("idb", [128, 128], BF16)
    gsel = A("gsel", [128, 2], BF16)
    wf = A("wf", [128, 2048], F32)
    wb = A("wb", [128, 2048], BF16)
    ab = A("ab", [128, 2048], BF16)
    qt = A("qt", [128, 2048], BF16)
    sc2 = A("sc2", [2, 2048], BF16)
    rc2 = A("rc2", [2, 2048], F32)
    rcrep = A("rcrep", [128, 2048], F32)
    screp = A("screp", [128, 2048], BF16)
    wqt = A("wqt", [128, 1, 2048], BF16)

    PA = nc.alloc_psum_tensor("PA", [128, 2048], F32)
    PB = nc.alloc_psum_tensor("PB", [128, 2048], F32)

    s_ld = P.sem("s_ld")
    s_gd = P.sem("s_gd")
    s_t = P.sem("s_t")
    s_sc = P.sem("s_sc")
    s_v = P.sem("s_v")
    s_g = P.sem("s_g")
    s_sp = P.sem("s_sp")
    s_st = P.sem("s_st")

    for dst, srcap in [(con, consts), (orow, onesrow), (ocolb, onescolb),
                       (idb, identb), (saL, saL_i), (saR, saR_i), (cbt, cb_i),
                       (gsel, gsel_i)]:
        P.dma("sync", dst[:, :], srcap, s_ld)
    for c in range(16):
        P.dma("sync", yb2[:, c, :], yb2_b[c], s_ld)
    ld0 = s_ld.n
    for e in ("tensor", "scalar", "vector", "gpsimd"):
        P.wait(e, s_ld, ld0)

    def emit_quant(nchunks, src_dr, out_dr, width_hint):
        for c in range(nchunks):
            buf = c % 2
            if c >= 2:
                P.wait("sync", s_v, quant_done[c - 2])
            if c >= 1:
                P.wait("sync", s_v, quant_done[c - 1])
            P.dma("sync", wf[:, :], src_dr(c), s_sp)
            wfd = s_sp.n
            P.wait("vector", s_sp, wfd)
            if c >= 1:
                P.wait("vector", s_st, stq_done[c - 1])
            P.op("vector", lambda v: v.tensor_copy(wb[:, :], wf[:, :]), sem=s_v)
            wbd = s_v.n
            P.wait("scalar", s_v, wbd)
            P.op("scalar", lambda s: s.activation(ab[:, :], wb[:, :], AF.Abs), sem=s_sc)
            abd = s_sc.n
            P.wait("tensor", s_sc, abd)
            r = None
            for w in range(4):
                r = P.op("tensor", lambda t, w=w: t.matmul(
                    PA.ap()[0:2, w * 512:(w + 1) * 512], gsel[:, :], ab[:, w * 512:(w + 1) * 512],
                    start=True, stop=True), sem=(s_t if w == 3 else None))
            P.wait("vector", s_t, r)
            P.op("vector", lambda v: v.tensor_single_scalar(
                sc2[0:2, :], PA.ap()[0:2, :], 1.0 / 64.0, ALU.mult), sem=None)
            P.op("vector", lambda v: v.tensor_single_scalar(
                sc2[0:2, :], sc2[0:2, :], 1e-8, ALU.max), sem=None)
            P.op("vector", lambda v: v.reciprocal(rc2[0:2, :], sc2[0:2, :]), sem=s_v)
            scd = s_v.n
            P.wait("sync", s_v, scd)
            P.dma("sync", scr_qs[0:2, :], sc2[0:2, :], s_sp)
            P.dma("sync", scr_q[0:2, :], rc2[0:2, :], s_sp)
            P.wait("sync", s_sp, s_sp.n)
            P.dma("sync", screp[:, :], bass.AP(scr_qs.tensor, 0, [[2048, 2], [0, 64], [1, 2048]]), s_sp)
            P.dma("sync", rcrep[:, :], bass.AP(scr_q.tensor, 0, [[2048, 2], [0, 64], [1, 2048]]), s_sp)
            rpd = s_sp.n
            P.wait("vector", s_sp, rpd)
            P.op("vector", lambda v: v.tensor_tensor(qt[:, :], wb[:, :], rcrep[:, :], ALU.mult), sem=None)
            P.op("vector", lambda v: v.tensor_single_scalar(wb[:, :], qt[:, :], 0.5, ALU.is_gt), sem=None)
            P.op("vector", lambda v: v.tensor_single_scalar(ab[:, :], qt[:, :], -0.5, ALU.is_lt), sem=None)
            P.op("vector", lambda v: v.tensor_tensor(wb[:, :], wb[:, :], ab[:, :], ALU.subtract), sem=None)
            P.op("vector", lambda v: v.tensor_tensor(
                wqt[:, 0, :], wb[:, :], screp[:, :], ALU.mult), sem=s_v)
            qd = s_v.n
            quant_done[c] = qd
            P.wait("sync", s_v, qd)
            P.dma("sync", out_dr(c), wqt[:, 0, :], s_st)
            stq_done[c] = s_st.n

    quant_done = {}
    stq_done = {}
    emit_quant(16, lambda c: wguT_sl[c], lambda c: wqgu_o[c], 2048)
    quant_done = {}
    stq_done = {}
    emit_quant(8, lambda c: wpT_sl[c], lambda c: wqp_o[c], 2048)
    quant_all = s_st.n

    P.wait("tensor", s_st, quant_all)
    P.wait("vector", s_st, quant_all)
    P.wait("scalar", s_st, quant_all)
    wqsrc = wq_sl.rearrange("(c p) m -> c p m", p=128)
    wkvsrc = wkv_sl.rearrange("(c p) m -> c p m", p=128)
    mmq = {}
    evq = {}
    for mt in range(16):
        buf = mt % 2
        if mt >= 2:
            P.wait("gpsimd", s_t, mmq[mt - 2])
        if mt < 8:
            P.dma("gpsimd", wcolq[:, buf, :, :],
                  wqsrc[:, :, mt * 128:(mt + 1) * 128].transpose([1, 0, 2]), s_gd)
        else:
            P.dma("gpsimd", wcolq[:, buf, :, :],
                  wkvsrc[:, :, (mt - 8) * 128:(mt - 7) * 128].transpose([1, 0, 2]), s_gd)
        wd = s_gd.n
        psum = PA.ap() if buf == 0 else PB.ap()
        P.wait("tensor", s_gd, wd)
        if mt >= 2:
            P.wait("tensor", s_v, evq[mt - 2])
        r = None
        for w in range(4):
            t0 = w * 512
            for k in range(16):
                r = P.op("tensor", lambda t, k=k, t0=t0, buf=buf, psum=psum: t.matmul(
                    psum[:, t0:t0 + 512], wcolq[:, buf, k, :], yb2[:, k, t0:t0 + 512],
                    start=(k == 0), stop=(k == 15)), sem=(s_t if (w == 3 and k == 15) else None))
        mmq[mt] = r
        P.wait("vector", s_t, r)
        if mt < 8:
            dstv = qT[:, mt, :]
        elif mt < 12:
            dstv = kT[:, mt - 8, :]
        else:
            dstv = vT[:, mt - 12, :]
        ev = P.op("vector", lambda v, dstv=dstv, psum=psum: v.tensor_copy(
            dstv, psum[:, :]), sem=s_v)
        evq[mt] = ev
    proj_done = s_v.n

    P.wait("tensor", s_v, proj_done)
    for kv in range(4):
        for wv in range(2):
            trl = None
            for j in range(8):
                sc = wv * 8 + j
                trl = P.op("tensor", lambda t, kv=kv, sc=sc, j=j: t.transpose(
                    PA.ap().bitcast(BF16)[:, 2048 + j * 128:2048 + (j + 1) * 128],
                    vT[:, kv, sc * 128:(sc + 1) * 128], idb[:, :]), sem=s_t)
            P.wait("vector", s_t, trl)
            cpl = None
            for j in range(8):
                sc = wv * 8 + j
                cpl = P.op("vector", lambda v, kv=kv, sc=sc, j=j: v.tensor_copy(
                    vtok[:, kv * 16 + sc, :], PA.ap().bitcast(BF16)[:, 2048 + j * 128:2048 + (j + 1) * 128]),
                    sem=s_v)
            P.wait("tensor", s_v, cpl)
    vtok_done = s_v.n

    P.wait("tensor", s_v, vtok_done)
    pdone = {}
    last_exp = [0]
    for hl in range(8):
        kv = (8 * 0 + hl) % 4 if False else hl % 4
        for w in range(4):
            t0 = w * 512
            nsc = 4 * w + 4
            avl = None
            csl = None
            for sc in range(nsc):
                pb = sc % 2
                P.wait("tensor", s_sc, last_exp[0])
                P.op("tensor", lambda t, kv=kv, sc=sc, hl=hl, t0=t0: t.matmul(
                    PA.ap()[:, 0:512], kT[:, kv, sc * 128:(sc + 1) * 128],
                    qT[:, hl, t0:t0 + 512], start=True, stop=False), sem=None)
                smm = P.op("tensor", lambda t, hl=hl, sc=sc: t.matmul(
                    PA.ap()[:, 0:512], saL[:, hl * 128:(hl + 1) * 128], saR[:, :],
                    start=False, stop=True), sem=s_t)
                P.wait("scalar", s_t, smm)
                if pdone.get((hl, w, sc - 2)):
                    P.wait("scalar", s_t, pdone[(hl, w, sc - 2)][1])
                idx = hl * 64 + sc * 4 + w
                exd = P.op("scalar", lambda s, pb=pb, idx=idx: s.activation(
                    ptile[:, pb, :], PA.ap()[:, 0:512], AF.Exp, bias=cbt[:, idx:idx + 1],
                    scale=float(2.0 ** -3.5)), sem=s_sc)
                last_exp[0] = exd
                diag = (sc >= t0 // 128) and (sc < t0 // 128 + 4)
                gmk = 0
                if diag:
                    P.wait("gpsimd", s_sc, exd)
                    gmk = P.op("gpsimd", lambda gg, pb=pb, t0=t0, sc=sc: gg.affine_select(
                        ptile[:, pb, :], ptile[:, pb, :], [[1, 512]], ALU.is_ge, 0.0,
                        base=t0 - sc * 128, channel_multiplier=-1), sem=s_g)
                    P.wait("tensor", s_g, gmk)
                else:
                    P.wait("tensor", s_sc, exd)
                csl = P.op("tensor", lambda t, pb=pb, sc=sc: t.matmul(
                    PB.ap()[0:1, 1536:2048], ocolb[:, 0:1], ptile[:, pb, :],
                    start=(sc == 0), stop=(sc == nsc - 1)), sem=s_t)
                avl = P.op("tensor", lambda t, pb=pb, kv=kv, sc=sc: t.matmul(
                    PB.ap()[:, 0:512], vtok[:, kv * 16 + sc, :], ptile[:, pb, :],
                    start=(sc == 0), stop=(sc == nsc - 1)), sem=s_t)
                pdone[(hl, w, sc)] = (csl, avl)
            P.wait("vector", s_t, avl)
            P.op("vector", lambda v: v.reciprocal(rrow[0:1, :], PB.ap()[0:1, 1536:2048]), sem=s_v)
            rr = s_v.n
            P.wait("tensor", s_v, rr)
            bcv = P.op("tensor", lambda t: t.matmul(
                PA.ap()[:, 512:1024], orow[0:1, :], rrow[0:1, :], start=True, stop=True), sem=s_t)
            P.wait("vector", s_t, bcv)
            P.op("vector", lambda v: v.tensor_copy(rbv[:, :], PA.ap()[:, 512:1024]), sem=None)
            ov = P.op("vector", lambda v: v.scalar_tensor_tensor(
                obuf[:, :], PB.ap()[:, 0:512], 1.0, rbv[:, :], ALU.mult, ALU.mult), sem=s_v)
            P.wait("sync", s_v, ov)
            P.dma("sync", oT_out[hl, :, t0:t0 + 512], obuf[:, :], s_st)
            st_o = s_st.n
            P.wait("vector", s_st, st_o)
            P.wait("tensor", s_v, ov)
    P.wait("sync", s_st, s_st.n)

    P.build()
    return nc


def host_D_inputs(inputs, core):
    b, H = core // 2, core % 2
    qcols = np.arange(1024 * H, 1024 * H + 1024)
    kcols = np.arange(256 * H, 256 * H + 256)
    vcols = 512 + np.arange(256 * H, 256 * H + 256)
    slopes = (2.0 ** (-2.0 ** (-(np.log2(16) - 3.0)))) ** (np.arange(16) + 1)
    myslopes = slopes[8 * H:8 * H + 8]
    saL = np.zeros((2, 1024), np.float32)
    for hl in range(8):
        saL[0, hl * 128:(hl + 1) * 128] = myslopes[hl] * np.arange(128) * np.sqrt(128.0)
        saL[1, hl * 128:(hl + 1) * 128] = myslopes[hl] * np.sqrt(128.0)
    saR = np.zeros((2, 512), np.float32)
    saR[0, :] = 1.0
    saR[1, :] = -np.arange(512)
    cb = np.zeros((128, 512), np.float32)
    for hl in range(8):
        for sc in range(16):
            for w in range(4):
                idx = hl * 64 + sc * 4 + w
                cb[:, idx] = myslopes[hl] * (sc * 128 - w * 512)
    gsel = np.zeros((128, 2), np.float32)
    gsel[0:64, 0] = 1.0
    gsel[64:128, 1] = 1.0
    consts = np.zeros((128, 8), np.float32)
    consts[:, 0] = 1.0
    consts[:, 1] = EPS
    consts[:, 2] = EPS32
    wguT = inputs["L1_ffn_w_gate_up"].T
    wpT = inputs["L1_ffn_w_proj"].T
    wgu_sl = f32c(wguT[:, 2048 * core:2048 * core + 2048]).reshape(16, 128, 2048)
    wp_sl = f32c(wpT[:, 256 * core:256 * core + 256]).reshape(8, 8, 128, 256)
    wp_sl = np.ascontiguousarray(wp_sl.transpose(0, 2, 1, 3).reshape(8, 128, 2048))
    return {
        "wq_sl": f32c(inputs["L1_attn_Wq"][:, qcols]),
        "wkv_sl": f32c(inputs["L1_attn_Wkv"]),
        "saL_i": saL, "saR_i": saR, "cb_i": cb,
        "wguT_sl": wgu_sl, "wpT_sl": wp_sl,
        "gsel_i": gsel.astype(BFD),
        "consts": consts,
        "onesrow": np.ones((1, 128), np.float32),
        "onescolb": np.ones((128, 1), np.float32).astype(BFD),
        "identb": np.eye(128, dtype=np.float32).astype(BFD),
    }



E_DEBUG = False


def build_E():
    nc = bass.Bass("TRN2", target_bir_lowering=False)
    P = Plan(nc)
    TOK = 1024

    aT_sl = nc.dram_tensor("aT_sl", [16, 128, TOK], BF16, kind="ExternalInput").ap()
    x2_sl = nc.dram_tensor("x2_sl", [16, 128, TOK], F32, kind="ExternalInput").ap()
    wo = nc.dram_tensor("wo", [DM, DM], F32, kind="ExternalInput").ap()
    bo_c = nc.dram_tensor("bo_c", [128, 16], F32, kind="ExternalInput").ap()
    wqgu_t = nc.dram_tensor("wqgu_t", [16, 128, 16384], BF16, kind="ExternalInput").ap()
    wqp_t = nc.dram_tensor("wqp_t", [64, 128, 2048], BF16, kind="ExternalInput").ap()
    consts = nc.dram_tensor("consts", [128, 8], F32, kind="ExternalInput").ap()
    onesrow = nc.dram_tensor("onesrow", [1, 128], F32, kind="ExternalInput").ap()
    onescolb = nc.dram_tensor("onescolb", [128, 1], BF16, kind="ExternalInput").ap()

    x4_out = nc.dram_tensor("x4_out", [16, 128, TOK], F32, kind="ExternalOutput").ap()
    x3_sp = nc.dram_tensor("x3_sp", [16, 128, TOK], F32, kind="Internal").ap()
    if E_DEBUG:
        dbg_x3 = nc.dram_tensor("dbg_x3", [16, 128, TOK], F32, kind="ExternalOutput").ap()
        dbg_yb3 = nc.dram_tensor("dbg_yb3", [16, 128, TOK], BF16, kind="ExternalOutput").ap()
        dbg_h = nc.dram_tensor("dbg_h", [4, 128, TOK], BF16, kind="ExternalOutput").ap()
        dbg_f1 = nc.dram_tensor("dbg_f1", [16, 128, TOK], BF16, kind="ExternalOutput").ap()
        dbg_a1 = nc.dram_tensor("dbg_a1", [16, 128, TOK], BF16, kind="ExternalOutput").ap()

    rem = nc.sbuf_bytes_remaining
    arena_h = nc.alloc_sbuf_tensor("arena", [128, ((rem - 1024) // 512) * 512], mybir.dt.uint8)
    ar_f32 = arena_h.ap().bitcast(F32)
    ar_bf = arena_h.ap().bitcast(BF16)

    def cv(off, dtype, shape):
        parts = shape[0]
        elems = 1
        for d in shape[1:]:
            elems *= d
        base = ar_f32 if dtype == F32 else ar_bf
        isz = 4 if dtype == F32 else 2
        v = base[0:parts, off // isz: off // isz + elems]
        if len(shape) == 2:
            return v
        if len(shape) == 3:
            return v.rearrange("p (a b) -> p a b", a=shape[1], b=shape[2])
        return v.rearrange("p (a b c) -> p a b c", a=shape[1], b=shape[2], c=shape[3])

    KB = 1024
    R1 = 0
    aT = cv(R1, BF16, [128, 16, TOK])
    a1r = cv(R1 + 32 * KB, BF16, [128, 16, TOK])
    hti = cv(R1, BF16, [128, 64, TOK])
    R2 = 64 * KB
    x3 = cv(R2, F32, [128, 16, TOK])
    xre = cv(R1, F32, [128, 16, TOK])
    R3 = 128 * KB
    x3b = cv(R3, BF16, [128, 16, TOK])
    f1r = cv(R3, BF16, [128, 16, TOK])
    R4 = 160 * KB
    yb3 = cv(R4, BF16, [128, 16, TOK])
    wpc = cv(R4, BF16, [128, 2, 64, 128])
    wgug = cv(R3, BF16, [128, 2, 16, 128])
    wguu = cv(R3 + 8 * KB, BF16, [128, 2, 16, 128])
    R5 = 192 * KB
    wcol = cv(R5, BF16, [128, 2, 16, 128])
    rows = cv(R5, F32, [1, 3, 512])
    PR = 198 * KB
    sgb = cv(PR, BF16, [128, TOK])
    xsq = cv(PR + 2 * KB, BF16, [128, 1024])
    lnsc = cv(PR + 4 * KB, F32, [128, 512])
    con = cv(PR + 6 * KB, F32, [128, 8])
    orow = cv(PR + 6 * KB + 512, F32, [1, 128])
    ocolb = cv(PR + 7 * KB, BF16, [128, 1])
    boc = cv(PR + 7 * KB + 512, F32, [128, 16])

    PA = nc.alloc_psum_tensor("PA", [128, 2048], F32)
    PB = nc.alloc_psum_tensor("PB", [128, 2048], F32)

    s_ld = P.sem("s_ld")
    s_gd = P.sem("s_gd")
    s_t = P.sem("s_t")
    s_sc = P.sem("s_sc")
    s_v = P.sem("s_v")
    s_sp = P.sem("s_sp")
    s_st = P.sem("s_st")

    g = dict(s_t=s_t, s_sc=s_sc, s_v=s_v, rows=rows, xsq=xsq, con=con,
             ones_colb=ocolb[:, 0:1], ones_row=orow[0:1, :], lnsc=lnsc,
             PBrow=PB.ap()[0:1, 0:1024], PBC=PB.ap()[:, 1024:2048])

    for dst, srcap in [(con, consts), (orow, onesrow), (ocolb, onescolb), (boc, bo_c)]:
        P.dma("sync", dst[:, :], srcap, s_ld)
    for c in range(16):
        P.dma("sync", aT[:, c, :], aT_sl[c], s_ld)
    for c in range(16):
        P.dma("sync", x3[:, c, :], x2_sl[c], s_ld)
    ld0 = s_ld.n
    for e in ("tensor", "scalar", "vector", "gpsimd"):
        P.wait(e, s_ld, ld0)

    wsrc = wo.rearrange("(c p) m -> c p m", p=128)
    mmw = {}
    evw = {}
    for mt in range(16):
        buf = mt % 2
        if mt >= 2:
            P.wait("gpsimd", s_t, mmw[mt - 2])
        P.dma("gpsimd", wcol[:, buf, :, :],
              wsrc[:, :, mt * 128:(mt + 1) * 128].transpose([1, 0, 2]), s_gd)
        wd = s_gd.n
        psum = PA.ap() if buf == 0 else PB.ap()
        P.wait("tensor", s_gd, wd)
        if mt >= 2:
            P.wait("tensor", s_v, evw[mt - 2])
        r = None
        for w in range(2):
            t0 = w * 512
            for k in range(16):
                r = P.op("tensor", lambda t, k=k, t0=t0, buf=buf, psum=psum: t.matmul(
                    psum[:, t0:t0 + 512], wcol[:, buf, k, :], aT[:, k, t0:t0 + 512],
                    start=(k == 0), stop=(k == 15)), sem=(s_t if (w == 1 and k == 15) else None))
        mmw[mt] = r
        P.wait("vector", s_t, r)
        ev = P.op("vector", lambda v, mt=mt, psum=psum: v.tensor_scalar(
            a1r[:, mt, :], psum[:, 0:1024], boc[:, mt:mt + 1], None, ALU.add), sem=s_v)
        evw[mt] = ev
    a1_done = s_v.n
    if E_DEBUG:
        P.wait("sync", s_v, a1_done)
        for c in range(16):
            P.dma("sync", dbg_a1[c], a1r[:, c, :], s_st)

    P.wait("scalar", s_v, a1_done)
    emit_ln(P, g, lambda c: a1r[:, c, :], 16, None, tok=TOK,
            res_src=lambda c, t0, t1: x3[:, c, t0:t1],
            res_dst=lambda c, t0, t1: x3[:, c, t0:t1],
            res_bf=lambda c, t0, t1: x3b[:, c, t0:t1])
    x3_done = s_v.n
    P.wait("scalar", s_v, x3_done)
    emit_ln(P, g, lambda c: x3b[:, c, :], 16,
            lambda c, t0, t1: yb3[:, c, t0:t1], tok=TOK)
    yb3_done = s_v.n
    P.wait("sync", s_v, yb3_done)
    for c in range(16):
        P.dma("sync", x3_sp[c], x3[:, c, :], s_sp)
    x3sp_done = s_sp.n
    if E_DEBUG:
        for c in range(16):
            P.dma("sync", dbg_x3[c], x3[:, c, :], s_sp)
            P.dma("sync", dbg_yb3[c], yb3[:, c, :], s_sp)
        x3sp_done = s_sp.n

    P.wait("gpsimd", s_sp, x3sp_done)
    P.wait("vector", s_sp, x3sp_done)
    P.wait("scalar", s_sp, x3sp_done)
    P.wait("tensor", s_sp, x3sp_done)
    mmg = {}
    evg = {}
    sgd_ = {}
    for i in range(64):
        buf = i % 2
        if i >= 2:
            P.wait("gpsimd", s_t, mmg[i - 2])
        P.dma("gpsimd", wgug[:, buf, :, :],
              wqgu_t[:, :, i * 128:(i + 1) * 128].transpose([1, 0, 2]), s_gd)
        P.dma("gpsimd", wguu[:, buf, :, :],
              wqgu_t[:, :, 8192 + i * 128:8192 + (i + 1) * 128].transpose([1, 0, 2]), s_gd)
        wd = s_gd.n
        P.wait("tensor", s_gd, wd)
        if i >= 1:
            P.wait("tensor", s_v, evg[i - 1])
            P.wait("tensor", s_sc, sgd_[i - 1])
        r = None
        for w in range(2):
            t0 = w * 512
            for k in range(16):
                P.op("tensor", lambda t, k=k, t0=t0, buf=buf: t.matmul(
                    PA.ap()[:, t0:t0 + 512], wgug[:, buf, k, :], yb3[:, k, t0:t0 + 512],
                    start=(k == 0), stop=(k == 15)), sem=None)
            for k in range(16):
                r = P.op("tensor", lambda t, k=k, t0=t0, buf=buf: t.matmul(
                    PB.ap()[:, t0:t0 + 512], wguu[:, buf, k, :], yb3[:, k, t0:t0 + 512],
                    start=(k == 0), stop=(k == 15)), sem=(s_t if (w == 1 and k == 15) else None))
        mmg[i] = r
        P.wait("scalar", s_t, r)
        sg = P.op("scalar", lambda s: s.activation(sgb[:, :], PA.ap()[:, 0:1024], AF.Silu), sem=s_sc)
        sgd_[i] = sg
        P.wait("vector", s_sc, sg)
        ev = P.op("vector", lambda v, i=i: v.scalar_tensor_tensor(
            hti[:, i, :], PB.ap()[:, 0:1024], 1.0, sgb[:, :], ALU.mult, ALU.mult), sem=s_v)
        evg[i] = ev
    h_done = s_v.n
    if E_DEBUG:
        P.wait("sync", s_v, h_done)
        for c in range(4):
            P.dma("sync", dbg_h[c], hti[:, c, :], s_st)

    P.wait("scalar", s_v, h_done)
    inv = 1.0 / 8192.0
    for w in range(2):
        t0 = w * 512
        P.wait("tensor", s_v, s_v.n)
        last_mm = [None]
        for c in range(64):
            sq = P.op("scalar", lambda s, c=c, t0=t0: s.activation(
                xsq[:, (c % 2) * 512:(c % 2) * 512 + 512], hti[:, c, t0:t0 + 512], AF.Square),
                sem=s_sc)
            P.wait("tensor", s_sc, sq)
            mm = P.op("tensor", lambda t, c=c: t.matmul(
                PB.ap()[0:1, 0:512], ocolb[:, 0:1],
                xsq[:, (c % 2) * 512:(c % 2) * 512 + 512],
                start=(c == 0), stop=(c == 63)), sem=s_t)
            if last_mm[0] is not None:
                P.wait("scalar", s_t, last_mm[0])
            last_mm[0] = mm
        P.wait("vector", s_t, s_t.n)
        P.op("vector", lambda v: v.tensor_single_scalar(
            rows[0:1, 0, :], PB.ap()[0:1, 0:512], inv, ALU.mult), sem=s_v)
        P.wait("scalar", s_v, s_v.n)
        P.op("scalar", lambda s: s.activation(
            rows[0:1, 1, :], rows[0:1, 0, :], AF.Ln, bias=con[0:1, 2:3]), sem=None)
        P.op("scalar", lambda s: s.activation(
            rows[0:1, 0, :], rows[0:1, 1, :], AF.Exp, scale=-0.5), sem=s_sc)
        P.wait("tensor", s_sc, s_sc.n)
        bcv = P.op("tensor", lambda t: t.matmul(
            PB.ap()[:, 1024:1536], orow[0:1, :], rows[0:1, 0, :], start=True, stop=True), sem=s_t)
        P.wait("vector", s_t, bcv)
        for c in range(64):
            P.op("vector", lambda v, c=c, t0=t0: v.scalar_tensor_tensor(
                hti[:, c, t0:t0 + 512], hti[:, c, t0:t0 + 512], 1.0,
                PB.ap()[:, 1024:1536], ALU.mult, ALU.mult), sem=s_v)
    hn_done = s_v.n

    P.wait("gpsimd", s_v, hn_done)
    mmp = {}
    evp = {}
    for mt in range(16):
        buf = mt % 2
        if mt >= 2:
            P.wait("gpsimd", s_t, mmp[mt - 2])
        P.dma("gpsimd", wpc[:, buf, :, :],
              wqp_t.rearrange("a p b -> p a b")[:, :, mt * 128:(mt + 1) * 128] if False else
              wqp_t[:, :, mt * 128:(mt + 1) * 128].transpose([1, 0, 2]), s_gd)
        wd = s_gd.n
        psum = PA.ap() if buf == 0 else PB.ap()
        P.wait("tensor", s_gd, wd)
        if mt >= 2:
            P.wait("tensor", s_v, evp[mt - 2])
        r = None
        for w in range(2):
            t0 = w * 512
            for k in range(64):
                r = P.op("tensor", lambda t, k=k, t0=t0, buf=buf, psum=psum: t.matmul(
                    psum[:, t0:t0 + 512], wpc[:, buf, k, :], hti[:, k, t0:t0 + 512],
                    start=(k == 0), stop=(k == 63)), sem=(s_t if (w == 1 and k == 63) else None))
        mmp[mt] = r
        P.wait("vector", s_t, r)
        ev = P.op("vector", lambda v, mt=mt, psum=psum: v.tensor_copy(
            f1r[:, mt, :], psum[:, 0:1024]), sem=s_v)
        evp[mt] = ev
    f1_done = s_v.n
    if E_DEBUG:
        P.wait("sync", s_v, f1_done)
        for c in range(16):
            P.dma("sync", dbg_f1[c], f1r[:, c, :], s_st)

    P.wait("sync", s_v, f1_done)
    for c in range(16):
        P.dma("sync", xre[:, c, :], x3_sp[c], s_sp)
    xrel = s_sp.n
    P.wait("vector", s_sp, xrel)
    P.wait("scalar", s_v, f1_done)
    emit_ln(P, g, lambda c: f1r[:, c, :], 16, None, tok=TOK,
            res_src=lambda c, t0, t1: xre[:, c, t0:t1],
            res_dst=lambda c, t0, t1: xre[:, c, t0:t1])
    x4_done = s_v.n
    P.wait("sync", s_v, x4_done)
    for c in range(16):
        P.dma("sync", x4_out[c], xre[:, c, :], s_st)
    P.wait("sync", s_st, s_st.n)

    P.build()
    return nc


def host_E_inputs(inputs, core, oT_full, x2_full, wqgu, wqp):
    b, hh = core // 2, core % 2
    t0 = hh * 1024
    consts = np.zeros((128, 8), np.float32)
    consts[:, 0] = 1.0
    consts[:, 1] = EPS
    consts[:, 2] = EPS32
    return {
        "aT_sl": np.ascontiguousarray(oT_full[b].reshape(16, 128, S)[:, :, t0:t0 + 1024]),
        "x2_sl": np.ascontiguousarray(x2_full[b].reshape(16, 128, S)[:, :, t0:t0 + 1024]),
        "wo": f32c(inputs["L1_attn_Wo"]),
        "bo_c": f32c(np.asarray(inputs["L1_attn_bo"]).reshape(16, 128).T),
        "wqgu_t": wqgu,
        "wqp_t": wqp,
        "consts": consts,
        "onesrow": np.ones((1, 128), np.float32),
        "onescolb": np.ones((128, 1), np.float32).astype(BFD),
    }
